# version 03
# baseline (speedup 1.0000x reference)
import numpy as np
import ml_dtypes

import concourse.bass as bass
import concourse.mybir as mybir
from concourse.bass_utils import run_bass_kernel_spmd

F32 = mybir.dt.float32
F16 = mybir.dt.float16
BF16 = mybir.dt.bfloat16
U8 = mybir.dt.uint8
AF = mybir.ActivationFunctionType
ALU = mybir.AluOpType
AX = mybir.AxisListType

RNE_C = float(np.float32(1.5 * 2.0**23))
QA = 32767.0
QW = 7.0
U16_CLIP = float(np.float32(65535.0) / np.float32(65536.0))

TOK, TOWN, CD = 4096, 2048, 1024
NH, HS, FI = 8, 64, 3072
NI, NT, NTO = 24, 32, 16


class Sem:
    def __init__(self, nc, es, name):
        self.h = es.enter_context(nc.semaphore(name))
        self.n = 0

    def inc(self, k=1):
        self.n += k
        return self.n


class Region:
    def __init__(self, arena, off, size):
        self.arena, self.off, self.size = arena, off, size

    def t(self, shape, dt, off=0):
        esz = mybir.dt.size(dt)
        assert (self.off + off) % esz == 0
        n = 1
        for s in shape[1:]:
            n *= s
        assert off + n * esz <= self.size, f"region overflow {off}+{n*esz}>{self.size}"
        h = self.arena.bitcast(dt)
        start = (self.off + off) // esz
        ap = h[0:shape[0], start:start + n]
        if len(shape) > 2:
            dims = " ".join(f"d{i}" for i in range(1, len(shape)))
            kw = {f"d{i}": shape[i] for i in range(1, len(shape))}
            ap = ap.rearrange(f"p ({dims}) -> p {dims}", **kw)
        return ap


P7SER = False


def build(stage=10, debug=()):
    from contextlib import ExitStack

    nc = bass.Bass("TRN2", target_bir_lowering=False)

    xbt = nc.dram_tensor("xb", [TOK, CD], F32, kind="ExternalInput")
    xot = nc.dram_tensor("xo", [TOWN, CD], F32, kind="ExternalInput")
    wqkvt = nc.dram_tensor("wqkv", [1536, CD], F32, kind="ExternalInput")
    wprojt = nc.dram_tensor("wproj", [CD, CD], F32, kind="ExternalInput")
    wgatet = nc.dram_tensor("wgate", [FI, CD], F32, kind="ExternalInput")
    wupt = nc.dram_tensor("wup", [FI, CD], F32, kind="ExternalInput")
    wdownt = nc.dram_tensor("wdown", [CD, FI], F32, kind="ExternalInput")
    pidxt = nc.dram_tensor("pidx", [128, 4], mybir.dt.int32, kind="ExternalInput")
    outt = nc.dram_tensor("out", [TOWN, CD], F32, kind="ExternalOutput")

    cc_in = [nc.dram_tensor(f"cc{i}_in", [1, 2], F32) for i in range(4)]
    cc_out = [nc.dram_tensor(f"cc{i}_out", [8, 2], F32) for i in range(4)]
    yx_in = nc.dram_tensor("yx_in", [1024, TOWN], BF16)
    yx_out = nc.dram_tensor("yx_out", [8 * 1024, TOWN], BF16)
    upspill = nc.dram_tensor("upspill", [FI, TOWN], BF16)

    dbg = {}

    def dbg_t(name, shape, dt=F32):
        if name in debug:
            dbg[name] = nc.dram_tensor("dbg_" + name, shape, dt, kind="ExternalOutput")

    dbg_t("xnT", [128, 8 * TOK])
    dbg_t("qT", [128, 4 * TOK])
    dbg_t("kT", [128, 4 * TOK])
    dbg_t("v", [128, NT * NH * 66])
    dbg_t("xq", [128, 8 * TOWN])
    dbg_t("yy", [128, 8 * TOWN])
    dbg_t("gr", [128, NI * TOWN])
    dbg_t("oraw", [128, NI * TOWN])
    dbg_t("wdt", [128, NI * CD])
    dbg_t("wlt", [128, 8 * 1536])
    dbg_t("wtl", [128, 4096])
    dbg_t("wsl", [128, 2048])
    dbg_t("dqsd", [128, 64])
    dbg_t("scal", [1, 64])
    dbg_t("lnm", [128, 96])
    dbg_t("rmx", [128, 24])

    es = ExitStack()

    ARENA = 205824
    arena = es.enter_context(nc.sbuf_tensor("arena", [128, ARENA], U8))
    S0 = Region(arena, 0, 8192)
    S1 = Region(arena, 8192, 32768)
    S2 = Region(arena, 40960, 32768)
    S3 = Region(arena, 73728, 32768)
    S4 = Region(arena, 106496, 32768)
    S5 = Region(arena, 139264, 33792)
    S6 = Region(arena, 173056, 24576)
    S7 = Region(arena, 197632, 8192)

    ident = S0.t([128, 128], BF16, 0)
    mask_ut = S0.t([128, 128], BF16, 256)
    identf = S0.t([128, 128], F32, 512)
    ones_b = S0.t([128, 128], BF16, 1024)
    biasneg = S0.t([128, 4], F32, 1280)
    scalars = S0.t([1, 64], F32, 1296)
    bcast = S0.t([128, 16], F32, 1552)
    dqs = S0.t([128, 32], F32, 1616)
    dqi = S0.t([128, 32], F32, 1744)
    amax_t = S0.t([128, 32], F32, 1872)
    runmax = S0.t([128, 8], F32, 2000)
    negm_t = S0.t([128, 32], F32, 2032)
    rstd_t = S0.t([128, 32], F32, 2160)
    lnm = S0.t([128, 4], F32, 2288)
    norm8 = S0.t([128, 8], F32, 2304)
    ones_f = S0.t([128, 128], F32, 2368)
    cbias = S0.t([128, 4], F32, 2880)
    mxb = S0.t([128, 8], BF16, 2896)
    bd2 = S0.t([128, 2], BF16, 2912)
    norm2 = S0.t([128, 8], F32, 2920)
    pidx = S0.t([128, 4], mybir.dt.int32, 2952)
    rowmaxd = S0.t([128, 256], F32, 3072)
    mxq8 = S0.t([128, 12], F32, 4096)
    rb4 = S0.t([128, 4], BF16, 4144)
    mskS = S0.t([128, 512], F16, 4160)
    dgtile = S0.t([128, 512], BF16, 4160)
    mask_ltneg = S0.t([128, 128], F16, 5184)
    mask_utneg = S0.t([128, 128], F16, 5440)

    xnT_lo = S1.t([128, 4, TOK], BF16)
    xnT_hi = S2.t([128, 4, TOK], BF16)

    def xnT(cc):
        return (xnT_lo if cc < 4 else xnT_hi)[:, cc % 4, :]

    xqT = S1.t([128, 8, TOWN], BF16)
    qT = S3.t([128, 4, TOK], BF16)
    kT = S4.t([128, 4, TOK], BF16)
    vaug = S5.t([128, NT, NH, 66], BF16)
    yproj = S5.t([128, 4, TOWN], BF16, 0)
    yrecv = S5.t([128, 4, TOWN], BF16, 16384)
    yy = S2.t([128, 8, TOWN], BF16)
    wlT = S6.t([128, 8, 1536], BF16)
    wl_src = S3.t([128, 12, CD], BF16)
    wpT = S6.t([128, 8, CD], BF16)
    ptst = S6.t([128, 2, 1024], BF16, 16384)
    upevA = [S6.t([128, TOWN], BF16, i * 4096) for i in range(2)]
    dqtile = [S6.t([128, CD], BF16, 8192 + i * 2048) for i in range(2)]
    wtile = [S6.t([128, CD], BF16, 12288 + i * 2048) for i in range(2)]
    upevB = [S6.t([128, TOWN], BF16, 16384 + i * 4096) for i in range(2)]
    b1sb = [S6.t([128, 512], F32, 20480 + i * 2048) for i in range(2)]

    sqscr = S7.t([128, 4096], BF16, 0)
    xstage = S5.t([128, 2, 4096], F32)
    d2scr = S7.t([128, CD], BF16, 0)
    xnst = [S7.t([128, CD], BF16, 2048 + i * 2048) for i in range(2)]
    xqscr = S7.t([128, TOWN], F32, 0)
    wpl = S7.t([128, 4096], BF16, 0)
    rowall = S7.t([128, 512], F32, 0)
    bcsb = S7.t([128, 512], F32, 2048)
    stmp = [S7.t([128, 1024], F16, 4096 + i * 2048) for i in range(2)]
    xt7 = [S7.t([128, CD], F32, i * 4096) for i in range(2)]
    wslab = [S7.t([128, CD], F32, i * 4096) for i in range(2)]
    ot10 = [S7.t([128, CD], F32, i * 4096) for i in range(2)]

    def graw(ci, dt=BF16):
        r, j = (S4, ci) if ci < 8 else ((S5, ci - 8) if ci < 16 else (S2, ci - 16))
        return r.t([128, 8, TOWN], dt)[:, j, :]

    wdT_a = S3.t([128, 16, CD], BF16)
    wdT_b = S6.t([128, 8, CD], BF16)

    def wdT(ci):
        return wdT_a[:, ci, :] if ci < 16 else wdT_b[:, ci - 16, :]

    scr = [S1.t([128, TOWN], F32, i * 8192) for i in range(3)]
    xt10 = [S1.t([128, CD], F32, 24576 + i * 4096) for i in range(2)]
    wd_dqt = [S1.t([128, CD], BF16, 24576 + i * 2048) for i in range(2)]

    pS = es.enter_context(nc.psum_tensor("pS", [128, 2048], F32))
    pY = es.enter_context(nc.psum_tensor("pY", [128, 512], F32))
    pZ = es.enter_context(nc.psum_tensor("pZ", [128, 512], F32))
    pT = es.enter_context(nc.psum_tensor("pT", [128, 1024], F32))
    pTb = pT.bitcast(BF16)
    pSb = pS.bitcast(BF16)
    pYb = pY.bitcast(BF16)
    pZb = pZ.bitcast(BF16)
    BANK8 = [pSb[:, 0:128], pSb[:, 1024:1152], pSb[:, 2048:2176], pSb[:, 3072:3200],
             pYb[:, 0:128], pZb[:, 0:128], pTb[:, 0:128], pTb[:, 1024:1152]]
    pSb_s4 = pSb.rearrange("p (b f) -> p b f", f=1024)[:, :, 0:128]
    pTb_s2 = pTb.rearrange("p (b f) -> p b f", f=1024)[:, :, 0:128]
    BANK4 = [pYb[:, 0:128], pZb[:, 0:128], pTb[:, 0:128], pTb[:, 1024:1152]]

    dsy = Sem(nc, es, "dsy")
    dgp = Sem(nc, es, "dgp")
    spe = Sem(nc, es, "spe")
    sact = Sem(nc, es, "sact")
    sdve = Sem(nc, es, "sdve")
    spool = Sem(nc, es, "spool")
    scol = [Sem(nc, es, f"scol{i}") for i in range(4)]
    scy = Sem(nc, es, "scy")

    RG = [list(range(8))]
    PL = {"pe": [], "act": [], "dve": [], "pool": [], "sync": []}

    def op(engine):
        def deco(fn):
            PL[engine].append(fn)
        return deco

    SC_M0, SC_B, SC_INV0, SC_S0 = 0, 1, 2, 3
    SC_MUP, SC_MGATE, SC_MG2, SC_MO = 4, 5, 6, 7
    SC_G = 8
    SC_T0, SC_T1 = 28, 29
    SC_SC = 32
    BC_INV0, BC_S0 = 0, 1
    BC_INVU, BC_SU, BC_INVG, BC_SG = 2, 3, 4, 5
    BC_INV3, BC_S3, BC_INVO, BC_SO = 6, 7, 8, 9

    @op("pool")
    def _(g):
        g.iota(ones_b, [[-1, 128]], base=0, channel_multiplier=1,
               allow_small_or_imprecise_dtypes=True)
        g.memset(ident, 0.0).then_inc(spool.h, 1)
    m_iota = spool.inc()

    @op("pool")
    def _(g):
        g.dma_start(out=wl_src,
                    in_=wqkvt.ap().rearrange("(r p) c -> p r c", p=128)).then_inc(dgp.h, 16)
    m_wl = dgp.inc(16)

    @op("dve")
    def _(v):
        v.wait_ge(spool.h, m_iota)
        v.tensor_single_scalar(ident, ones_b, 0.0, ALU.is_equal)
        v.tensor_single_scalar(mask_ut, ones_b, 0.0, ALU.is_le)
        v.tensor_copy(identf, ident)
        v.tensor_scalar(mask_utneg, mask_ut, -1.0, 30000.0, ALU.add, ALU.mult)
        v.tensor_tensor(mask_ltneg, mask_ut, ident, ALU.subtract)
        v.tensor_scalar(mask_ltneg, mask_ltneg, -30000.0, 0.0, ALU.mult, ALU.add)
        v.memset(ones_b, 1.0)
        v.memset(ones_f, 1.0)
        v.memset(cbias[:, 0:1], 1e-5)
        v.memset(cbias[:, 1:2], RNE_C)
        v.memset(cbias[:, 2:3], -RNE_C / 65536.0)
        v.memset(bd2, 0.0)
        v.memset(bd2[0:64, 0:1], 1.0)
        v.memset(bd2[64:128, 1:2], 1.0)
        v.memset(norm2, 0.0)
        v.memset(scalars, 0.0)
        v.memset(norm8, 0.0)
        v.memset(runmax, 0.0).then_inc(sdve.h, 1)
    m_const = sdve.inc()

    m_wlT_ev = []
    for rc in range(12):
        @op("pe")
        def _(t, rc=rc, w=(m_wlT_ev[rc - 1] if rc >= 1 else 0)):
            if rc == 0:
                t.wait_ge(dgp.h, m_wl)
                t.wait_ge(sdve.h, m_const)
            if w:
                t.wait_ge(sdve.h, w)
            for cc in range(8):
                ii = t.transpose(BANK8[cc], wl_src[:, rc, cc * 128:cc * 128 + 128],
                                 ident)
            ii.then_inc(spe.h, 1)
        m_tr = spe.inc()

        @op("dve")
        def _(v, rc=rc, m_tr=m_tr):
            v.wait_ge(spe.h, m_tr)
            rng = slice(rc * 128, rc * 128 + 128)
            for j in range(8):
                ii = v.tensor_copy(wlT[:, j, rng], BANK8[j])
            ii.then_inc(sdve.h, 1)
        m_wlT_ev.append(sdve.inc())
    m_wlT = m_wlT_ev[-1]

    if stage < 0.5:
        return _emit(nc, es, PL, dbg, dict(locals())), dbg

    b_dve, b_act, b_pe, b_dsy = sdve.n, sact.n, spe.n, dsy.n
    mX = [b_dsy + 16 * (st + 1) for st in range(8)]
    mVs, mAd, mVx, mPe, mVe = [0] * NT, [0] * NT, [0] * NT, [0] * NT, [0] * NT
    vd = b_dve
    for it in range(NT + 2):
        if it < NT:
            vd += 2; mVs[it] = vd
        if 1 <= it < NT + 1:
            vd += 1; mVx[it - 1] = vd
        if it >= 2:
            vd += 1; mVe[it - 2] = vd
    for tt in range(NT):
        mAd[tt] = b_act + tt + 1
        mPe[tt] = b_pe + tt + 1
    dsy.n = mX[-1]
    sdve.n, sact.n, spe.n = vd, b_act + NT, b_pe + NT

    @op("sync")
    def _(s):
        for st in range(8):
            if st >= 2:
                s.wait_ge(sact.h, mAd[4 * (st - 2) + 3])
                s.wait_ge(sdve.h, mVx[4 * (st - 2) + 3])
            s.dma_start(
                out=xstage[:, st % 2, :].rearrange("p (n c) -> p n c", n=4),
                in_=xbt.ap()[st * 512:(st + 1) * 512, :]
                    .rearrange("(n p) c -> p n c", p=128),
            ).then_inc(dsy.h, 16)

    @op("dve")
    def _(v):
        for it in range(NT + 2):
            if it < NT:
                st, j = it // 4, it % 4
                xt = xstage[:, st % 2, j * CD:(j + 1) * CD]
                v.wait_ge(dsy.h, mX[st])
                v.tensor_reduce(lnm[:, 0:1], xt, AX.X, ALU.add, negate=True)\
                    .then_inc(sdve.h, 1)
                v.wait_ge(sdve.h, mVs[it] - 1)
                v.tensor_scalar_mul(negm_t[:, it:it + 1], lnm[:, 0:1], 1.0 / CD)\
                    .then_inc(sdve.h, 1)
            if 1 <= it < NT + 1:
                t1 = it - 1
                st1, j1 = t1 // 4, t1 % 4
                xt1 = xstage[:, st1 % 2, j1 * CD:(j1 + 1) * CD]
                v.wait_ge(sact.h, mAd[t1])
                if t1 >= 2:
                    v.wait_ge(spe.h, mPe[t1 - 2])
                v.tensor_scalar(xnst[t1 % 2], xt1, negm_t[:, t1:t1 + 1],
                                rstd_t[:, t1:t1 + 1], ALU.add, ALU.mult)\
                    .then_inc(sdve.h, 1)
                v.tensor_reduce(amax_t[:, t1:t1 + 1], xnst[t1 % 2], AX.X, ALU.max,
                                apply_absolute_value=True)
            if it >= 2:
                t2 = it - 2
                v.wait_ge(spe.h, mPe[t2])
                rng = slice(t2 * 128, t2 * 128 + 128)
                for j in range(4):
                    v.tensor_copy(xnT_lo[:, j, rng], BANK8[j])
                for j in range(4):
                    ii = v.tensor_copy(xnT_hi[:, j, rng], BANK8[4 + j])
                ii.then_inc(sdve.h, 1)

    @op("act")
    def _(a):
        for tt in range(NT):
            st, j = tt // 4, tt % 4
            xt = xstage[:, st % 2, j * CD:(j + 1) * CD]
            a.wait_ge(sdve.h, mVs[tt])
            a.activation(d2scr, xt, AF.Square, bias=negm_t[:, tt:tt + 1],
                         accum_out=lnm[:, 2:3])
            a.activation(lnm[:, 3:4], lnm[:, 2:3], AF.Ln, scale=1.0 / CD, bias=cbias[:, 0:1])
            a.activation(rstd_t[:, tt:tt + 1], lnm[:, 3:4], AF.Exp, scale=-0.5)\
                .then_inc(sact.h, 1)

    @op("pe")
    def _(t):
        for tt in range(NT):
            t.wait_ge(sdve.h, mVx[tt])
            if tt >= 1:
                t.wait_ge(sdve.h, mVe[tt - 1])
            for cc in range(8):
                ii = t.transpose(BANK8[cc], xnst[tt % 2][:, cc * 128:cc * 128 + 128],
                                 ident)
            ii.then_inc(spe.h, 1)

    m_ln = mVe[-1]

    if stage < 0.8:
        return _emit(nc, es, PL, dbg, dict(locals())), dbg

    @op("dve")
    def _(v, base=sdve.n):
        v.tensor_reduce(lnm[:, 0:1], amax_t, AX.X, ALU.max).then_inc(sdve.h, 1)
        v.wait_ge(sdve.h, base + 1)
        v.tensor_copy(mxb[:, 0:1], lnm[:, 0:1]).then_inc(sdve.h, 1)
    m_am = sdve.inc(2)

    @op("pe")
    def _(t, w=m_am):
        t.wait_ge(sdve.h, w)
        t.transpose(pTb[0:1, 0:128], mxb[:, 0:1], ident).then_inc(spe.h, 1)
    m_amt = spe.inc()

    @op("dve")
    def _(v, w=m_amt):
        v.wait_ge(spe.h, w)
        v.tensor_reduce(scalars[:, SC_M0:SC_M0 + 1], pTb[0:1, 0:128], AX.X, ALU.max)\
            .then_inc(sdve.h, 1)
    m_m0p = sdve.inc()

    def collective(idx, src_ap, n, w_sem, w_val):
        @op("pool")
        def _(g, idx=idx, src_ap=src_ap, w_sem=w_sem, w_val=w_val):
            g.wait_ge(w_sem.h, w_val)
            g.dma_start(out=cc_in[idx].ap(), in_=src_ap).then_inc(dgp.h, 16)
        m_in = dgp.inc(16)

        @op("pool")
        def _(g, m_in=m_in, idx=idx):
            g.wait_ge(dgp.h, m_in)
            g.collective_compute(
                "AllGather", ALU.bypass, replica_groups=RG,
                ins=[cc_in[idx].ap().opt()],
                outs=[cc_out[idx].ap().opt()],
            ).then_inc(scol[idx].h, 1)
        scol[idx].inc()

        @op("pool")
        def _(g, idx=idx):
            g.wait_ge(scol[idx].h, 1)
            g.dma_start(out=scalars[:, SC_G:SC_G + 16],
                        in_=cc_out[idx].ap().rearrange("r c -> (r c)").unsqueeze(0))\
                .then_inc(dgp.h, 16)
        return dgp.inc(16)

    m_c1 = collective(0, scalars[:, SC_M0:SC_M0 + 2], 2, sdve, m_m0p)

    if stage < 1.2:
        return _emit(nc, es, PL, dbg, dict(locals())), dbg

    qk_ev = [0] * 64
    for m in range(8):
        for n in range(8):
            i = m * 8 + n

            @op("pe")
            def _(t, m=m, n=n, i=i, w=(qk_ev[i - 4] if i >= 4 else 0)):
                if i == 0:
                    t.wait_ge(sdve.h, m_wlT)
                    t.wait_ge(sdve.h, m_ln)
                if w:
                    t.wait_ge(sact.h, w)
                bank = i % 4
                for k in range(8):
                    ii = t.matmul(pS[:, bank * 512:bank * 512 + 512],
                                  wlT[:, k, m * 128:m * 128 + 128],
                                  xnT(k)[:, n * 512:n * 512 + 512],
                                  start=(k == 0), stop=(k == 7))
                ii.then_inc(spe.h, 1)
            m_mm = spe.inc()

            @op("act")
            def _(a, m=m, n=n, i=i, m_mm=m_mm):
                a.wait_ge(spe.h, m_mm)
                bank = i % 4
                dstT = qT if m < 4 else kT
                a.copy(dstT[:, m % 4, n * 512:n * 512 + 512],
                       pS[:, bank * 512:bank * 512 + 512]).then_inc(sact.h, 1)
            qk_ev[i] = sact.inc()

    if stage < 1.6:
        m_v = sdve.n
        m_qk = qk_ev[-1]
        return _emit(nc, es, PL, dbg, dict(locals())), dbg

    v_ev = [0] * NT
    for tt in range(NT):
        @op("pe")
        def _(t, tt=tt, w=(v_ev[tt - 2] if tt >= 2 else 0)):
            if w:
                t.wait_ge(sdve.h, w)
            bank = pY if tt % 2 == 0 else pZ
            for k in range(8):
                ii = t.matmul(bank[:, 0:512],
                              xnT(k)[:, tt * 128:tt * 128 + 128],
                              wlT[:, k, 1024:1536],
                              start=(k == 0), stop=(k == 7))
            ii.then_inc(spe.h, 1)
        m_mm = spe.inc()

        @op("dve")
        def _(v, tt=tt, m_mm=m_mm):
            v.wait_ge(spe.h, m_mm)
            bank = pY if tt % 2 == 0 else pZ
            v.tensor_copy(vaug[:, tt, :, 0:64],
                          bank[:, 0:512].rearrange("p (h d) -> p h d", h=8))\
                .then_inc(sdve.h, 1)
        v_ev[tt] = sdve.inc()

    @op("dve")
    def _(v):
        v.memset(vaug[:, :, :, 65:66], 0.0)
        v.memset(vaug[:, :, :, 64:65], 1.0).then_inc(sdve.h, 1)
    m_v = sdve.inc()
    m_qk = qk_ev[-1]

    if stage < 2.5:
        return _emit(nc, es, PL, dbg, dict(locals())), dbg

    @op("dve")
    def _(v, w=m_c1, base=sdve.n):
        v.wait_ge(dgp.h, w)
        v.tensor_reduce(scalars[:, SC_M0:SC_M0 + 1],
                        scalars[:, SC_G:SC_G + 16]
                            .rearrange("o (r c) -> o c r", r=8)[:, 0:1, :],
                        AX.X, ALU.max).then_inc(sdve.h, 1)
        v.wait_ge(sdve.h, base + 1)
        v.tensor_scalar(scalars[:, SC_S0:SC_S0 + 1], scalars[:, SC_M0:SC_M0 + 1],
                        1.01 / QA, 1e-12, ALU.mult, ALU.max).then_inc(sdve.h, 1)
        v.wait_ge(sdve.h, base + 2)
        v.reciprocal(scalars[:, SC_INV0:SC_INV0 + 1], scalars[:, SC_S0:SC_S0 + 1])\
            .then_inc(sdve.h, 1)
    m_s0 = sdve.inc(3)

    @op("pe")
    def _(t, w=m_s0):
        t.wait_ge(sdve.h, w)
        t.matmul(pZ[:, 0:2], ones_f[0:1, 0:128], scalars[:, SC_INV0:SC_INV0 + 2],
                 start=True, stop=True).then_inc(spe.h, 1)
    m_bc0 = spe.inc()

    @op("dve")
    def _(v, w=m_bc0):
        v.wait_ge(spe.h, w)
        v.tensor_copy(bcast[:, BC_INV0:BC_INV0 + 2], pZ[:, 0:2]).then_inc(sdve.h, 1)
    m_bc0c = sdve.inc()

    @op("dve")
    def _(v):
        rr = v.alloc_register("xq_rr")
        ro = v.alloc_register("xq_ro")
        v.cc_rank_ld(rr, replica_groups=RG)
        v.reg_alu(rr, rr, 2, ALU.mod)
        v.reg_mul(rr, rr, TOWN)
        for cc in range(8):
            src = xnT(cc)
            v.reg_add(ro, rr, src.offset)
            srcap = bass.AP(src.tensor, ro, [list(src.ap[0]), [1, TOWN]])
            v.tensor_scalar(xqscr, srcap, bcast[:, BC_INV0:BC_INV0 + 1], RNE_C,
                            ALU.mult, ALU.add)
            ii = v.tensor_scalar(xqT[:, cc, :], xqscr, RNE_C,
                                 bcast[:, BC_S0:BC_S0 + 1], ALU.subtract, ALU.mult)
        v.free_register(rr)
        v.free_register(ro)
        ii.then_inc(sdve.h, 1)
    m_xq = sdve.inc()

    if stage < 3:
        return _emit(nc, es, PL, dbg, dict(locals())), dbg

    sq_m = [0] * 8
    mm_m = {}
    red_m = {}
    for i in range(8):
        @op("act")
        def _(a, i=i, w=(mm_m.get((i - 1, 7), 0))):
            if i == 0:
                a.wait_ge(sdve.h, m_xq)
                a.wait_ge(sdve.h, m_v)
            if w:
                a.wait_ge(spe.h, w)
            src = (qT if i < 4 else kT)[:, i % 4, :]
            a.activation(sqscr, src, AF.Square).then_inc(sact.h, 1)
        sq_m[i] = sact.inc()

        for j in range(8):
            @op("pe")
            def _(t, i=i, j=j, w=sq_m[i], wb=(red_m.get((i, j - 2), 0) if j >= 2
                                              else red_m.get((i - 1, 6 + j), 0))):
                t.wait_ge(sact.h, w)
                if wb:
                    t.wait_ge(sdve.h, wb)
                bank = pY if j % 2 == 0 else pZ
                t.matmul(bank[0:2, 0:512], bd2, sqscr[:, j * 512:j * 512 + 512],
                         start=True, stop=True).then_inc(spe.h, 1)
            mm_m[(i, j)] = spe.inc()

            @op("dve")
            def _(v, i=i, j=j, w=mm_m[(i, j)], base=sdve.n):
                v.wait_ge(spe.h, w)
                bank = pY if j % 2 == 0 else pZ
                v.tensor_reduce(lnm[0:2, 2:3], bank[0:2, 0:512], AX.X, ALU.max)\
                    .then_inc(sdve.h, 1)
                v.wait_ge(sdve.h, base + 1)
                v.tensor_tensor(norm2[0:2, i:i + 1], norm2[0:2, i:i + 1],
                                lnm[0:2, 2:3], ALU.max).then_inc(sdve.h, 1)
            red_m[(i, j)] = sdve.inc(2)

    @op("dve")
    def _(v, w=red_m[(7, 7)]):
        v.wait_ge(sdve.h, w)
        v.tensor_copy(mxb[0:2, 0:8], norm2[0:2, 0:8]).then_inc(sdve.h, 1)
    m_n8c = sdve.inc()

    @op("pe")
    def _(t, w=m_n8c):
        t.wait_ge(sdve.h, w)
        t.transpose(pTb[0:8, 0:128][0:8, 0:2], mxb[0:2, 0:8], ident[0:2, 0:2])\
            .then_inc(spe.h, 1)
    m_n8 = spe.inc()

    @op("dve")
    def _(v, w=m_n8, base=sdve.n):
        v.wait_ge(spe.h, w)
        v.tensor_reduce(norm8[0:8, 0:1], pTb[0:8, 0:2], AX.X, ALU.max)\
            .then_inc(sdve.h, 1)
        v.wait_ge(sdve.h, base + 1)
        v.tensor_copy(mxb[0:8, 1:2], norm8[0:8, 0:1]).then_inc(sdve.h, 1)
    m_n8r = sdve.inc(2)

    @op("pe")
    def _(t, w=m_n8r):
        t.wait_ge(sdve.h, w)
        t.transpose(pTb[0:1, 128:136], mxb[0:8, 1:2], ident[0:8, 0:8])\
            .then_inc(spe.h, 1)
    m_n1 = spe.inc()

    @op("dve")
    def _(v, w=m_n1, base=sdve.n):
        v.wait_ge(spe.h, w)
        v.tensor_reduce(scalars[:, SC_T0:SC_T0 + 1], pTb[0:1, 128:132], AX.X, ALU.max)
        v.tensor_reduce(scalars[:, SC_T1:SC_T1 + 1], pTb[0:1, 132:136], AX.X, ALU.max)\
            .then_inc(sdve.h, 1)
        v.wait_ge(sdve.h, base + 1)
        v.tensor_tensor(scalars[:, SC_B:SC_B + 1], scalars[:, SC_T0:SC_T0 + 1],
                        scalars[:, SC_T1:SC_T1 + 1], ALU.mult).then_inc(sdve.h, 1)
    m_bb = sdve.inc(2)

    @op("act")
    def _(a, w=m_bb, base=sact.n):
        a.wait_ge(sdve.h, w)
        a.activation(scalars[:, SC_B:SC_B + 1], scalars[:, SC_B:SC_B + 1],
                     AF.Sqrt, scale=1.0 / (64.0 * 64.0)).then_inc(sact.h, 1)
        a.wait_ge(sact.h, base + 1)
        a.mul(scalars[:, SC_B:SC_B + 1], scalars[:, SC_B:SC_B + 1], -8.0 * 1.02)\
            .then_inc(sact.h, 1)
    m_bn = sact.inc(2)

    @op("pe")
    def _(t, w=m_bn):
        t.wait_ge(sact.h, w)
        t.matmul(pZ[:, 0:1], ones_f[0:1, 0:128], scalars[:, SC_B:SC_B + 1],
                 start=True, stop=True).then_inc(spe.h, 1)
    m_bcb = spe.inc()

    @op("dve")
    def _(v, w=m_bcb):
        v.wait_ge(spe.h, w)
        v.tensor_copy(biasneg[:, 0:1], pZ[:, 0:1]).then_inc(sdve.h, 1)
    m_bias = sdve.inc()

    wp_ev = []
    for hlf in range(2):
        @op("pool")
        def _(g, hlf=hlf, w=(m_bias if hlf == 0 else 0),
              wprev=(wp_ev[3] if hlf == 1 else 0)):
            if hlf == 0:
                g.wait_ge(sdve.h, w)
            else:
                g.wait_ge(sdve.h, wprev)
            g.dma_start(out=wpl.rearrange("p (r c) -> p r c", r=4),
                        in_=wprojt.ap()[hlf * 512:(hlf + 1) * 512, :]
                            .rearrange("(r p) c -> p r c", p=128))\
                .then_inc(dgp.h, 16)
        m_wpl = dgp.inc(16)

        for rc in range(4):
            rcg = hlf * 4 + rc

            @op("pe")
            def _(t, rc=rc, rcg=rcg, m_wpl=m_wpl,
                  w=(wp_ev[rcg - 1] if rcg >= 1 else 0)):
                if rcg == 0:
                    t.wait_ge(sact.h, m_qk)
                    t.wait_ge(sdve.h, m_v)
                if rc == 0:
                    t.wait_ge(dgp.h, m_wpl)
                if w:
                    t.wait_ge(sdve.h, w)
                for cc in range(8):
                    ii = t.transpose(
                        BANK8[cc],
                        wpl[:, rc * 1024 + cc * 128:rc * 1024 + cc * 128 + 128], ident)
                ii.then_inc(spe.h, 1)
            m_tr = spe.inc()

            @op("dve")
            def _(v, rcg=rcg, m_tr=m_tr):
                v.wait_ge(spe.h, m_tr)
                rng = slice(rcg * 128, rcg * 128 + 128)
                for j in range(8):
                    ii = v.tensor_copy(wpT[:, j, rng], BANK8[j])
                ii.then_inc(sdve.h, 1)
            wp_ev.append(sdve.inc())
    m_wpT = wp_ev[-1]

    mxp_red = {}
    mxt = 0
    for h in range(NH):
        hb = (h % 2) * 64
        hc = h // 2
        for qt in range(32):
            jd = qt % 4
            nf = qt // 4
            ncol = 0
            for kt5 in range(nf + 1):
                diagt = (kt5 == nf)
                mov = (jd + 1) * 128 if diagt else 512

                @op("pe")
                def _(t, hb=hb, hc=hc, qt=qt, kt5=kt5, mov=mov, mxt=mxt,
                      wfree=mxp_red.get(mxt - 4, 0)):
                    if mxt == 0:
                        t.wait_ge(sact.h, m_qk)
                        t.wait_ge(sdve.h, m_wpT)
                    if wfree:
                        t.wait_ge(sdve.h, wfree)
                    t.matmul(pS[:, (mxt % 4) * 512:(mxt % 4) * 512 + mov],
                             qT[hb:hb + 64, hc, qt * 128:qt * 128 + 128],
                             kT[hb:hb + 64, hc, kt5 * 512:kt5 * 512 + mov],
                             start=True, stop=True).then_inc(spe.h, 1)
                m_mm = spe.inc()

                @op("dve")
                def _(v, jd=jd, diagt=diagt, mov=mov, mxt=mxt, m_mm=m_mm,
                      ncol=ncol):
                    v.wait_ge(spe.h, m_mm)
                    bank = pS[:, (mxt % 4) * 512:(mxt % 4) * 512 + mov]
                    if not diagt:
                        v.tensor_reduce(mxq8[:, ncol:ncol + 1], bank, AX.X,
                                        ALU.max, apply_absolute_value=False)\
                            .then_inc(sdve.h, 1)
                    else:
                        nn = ncol
                        if jd > 0:
                            v.tensor_reduce(mxq8[:, nn:nn + 1],
                                            bank[:, 0:jd * 128], AX.X, ALU.max,
                                            apply_absolute_value=False)
                            nn += 1
                        v.tensor_tensor(mskS[:, 0:128],
                                        bank[:, jd * 128:jd * 128 + 128],
                                        mask_ltneg, ALU.add)
                        v.tensor_reduce(mxq8[:, nn:nn + 1], mskS[:, 0:128],
                                        AX.X, ALU.max,
                                        apply_absolute_value=False)\
                            .then_inc(sdve.h, 1)
                mxp_red[mxt] = sdve.inc()
                ncol += 1 if not diagt else (2 if jd > 0 else 1)
                mxt += 1

            @op("dve")
            def _(v, h=h, qt=qt, ncol=ncol, base=sdve.n):
                v.wait_ge(sdve.h, base)
                v.tensor_reduce(rowmaxd[:, h * 32 + qt:h * 32 + qt + 1],
                                mxq8[:, 0:ncol], AX.X, ALU.max,
                                apply_absolute_value=False).then_inc(sdve.h, 1)
            sdve.inc()
    m_rmx = sdve.n

    if stage < 4:
        return _emit(nc, es, PL, dbg, dict(locals())), dbg

    slot_exp, slot_av = {}, {}
    slot_ctr = 0
    prev_tt, prev_bcc = 0, 0
    p4regs = {}

    @op("dve")
    def _(v):
        rr = v.alloc_register("p4_rr")
        rp = v.alloc_register("p4_rp")
        ro = v.alloc_register("p4_ro")
        v.cc_rank_ld(rr, replica_groups=RG)
        v.reg_alu(rr, rr, 2, ALU.mod)
        v.reg_mul(rp, rr, 4 * TOWN)
        p4regs.update(rp=rp, ro=ro)
        v.nop().then_inc(sdve.h, 1)
    m_p4r = sdve.inc()

    slot_stt = {}
    for h in range(NH):
        even = (h % 2 == 0)
        for qc in range(8):
            s2 = (h * 8 + qc) % 2

            @op("dve")
            def _(v, h=h, qc=qc,
                  w=(m_rmx if (h == 0 and qc == 0) else 0), base=sdve.n):
                if w:
                    v.wait_ge(sdve.h, w)
                v.tensor_scalar(rb4,
                                rowmaxd[:, h * 32 + qc * 4:h * 32 + qc * 4 + 4],
                                0.125, 0.0, ALU.mult, ALU.add)\
                    .then_inc(sdve.h, 1)
                v.wait_ge(sdve.h, base + 1)
                v.tensor_tensor(dgtile.rearrange("p (a b) -> p a b", b=128),
                                ident[:, None, :].broadcast_to((128, 4, 128)),
                                rb4[:, :, None].broadcast_to((128, 4, 128)),
                                ALU.mult).then_inc(sdve.h, 1)
            m_dg = sdve.inc(2)

            @op("pe")
            def _(t, m_dg=m_dg, w_pz=prev_bcc):
                t.wait_ge(sdve.h, m_dg)
                if w_pz:
                    t.wait_ge(sdve.h, w_pz)
                t.matmul(pZ[:, 0:512], ones_b, dgtile, start=True, stop=True)\
                    .then_inc(spe.h, 1)
            m_b1mm = spe.inc()

            @op("dve")
            def _(v, s2=s2, m_b1mm=m_b1mm):
                v.wait_ge(spe.h, m_b1mm)
                v.tensor_scalar(b1sb[s2], pZ[:, 0:512], 8.0, 0.0,
                                ALU.add, ALU.add).then_inc(sdve.h, 1)
            sdve.inc()

            nkt = 4 * qc + 4
            first_av = True
            for p in range(nkt // 2):
                kts = [2 * p, 2 * p + 1]
                s_ = slot_ctr % 2
                w_psfree = slot_stt.get(slot_ctr - 2, 0)
                w_stfree = slot_exp.get(slot_ctr - 2, 0)
                w_slotfree = slot_av.get(slot_ctr - 2, 0)

                @op("pe")
                def _(t, h=h, qc=qc, kts=kts, s_=s_, w_psfree=w_psfree,
                      first=(slot_ctr == 0)):
                    if first:
                        t.wait_ge(sdve.h, m_bias)
                        t.wait_ge(sdve.h, m_wpT)
                    if w_psfree:
                        t.wait_ge(sdve.h, w_psfree)
                    for j, kt in enumerate(kts):
                        ii = t.matmul(
                            pS[:, s_ * 1024 + j * 512:s_ * 1024 + j * 512 + 512],
                            kT[(h % 2) * 64:(h % 2) * 64 + 64, h // 2,
                               kt * 128:kt * 128 + 128],
                            qT[(h % 2) * 64:(h % 2) * 64 + 64, h // 2,
                               qc * 512:qc * 512 + 512],
                            start=True, stop=True)
                    ii.then_inc(spe.h, 1)
                m_s = spe.inc()

                @op("dve")
                def _(v, qc=qc, kts=kts, s_=s_, s2=s2, m_s=m_s, nkt=nkt,
                      w_stfree=w_stfree):
                    v.wait_ge(spe.h, m_s)
                    if w_stfree:
                        v.wait_ge(sact.h, w_stfree)
                    ii = v.scalar_tensor_tensor(
                        stmp[s_].rearrange("p (a b) -> p a b", b=512),
                        pS[:, s_ * 1024:s_ * 1024 + 1024]
                            .rearrange("p (a b) -> p a b", b=512),
                        0.125,
                        b1sb[s2][:, None, :].broadcast_to((128, 2, 512)),
                        ALU.mult, ALU.subtract)
                    for kt in kts:
                        if kt >= nkt - 4:
                            o = (kt % 2) * 512 + 128 * (kt - 4 * qc)
                            ii = v.tensor_tensor(stmp[s_][:, o:o + 128],
                                                 stmp[s_][:, o:o + 128],
                                                 mask_utneg, ALU.add)
                    ii.then_inc(sdve.h, 1)
                m_stt = sdve.inc()
                slot_stt[slot_ctr] = m_stt

                @op("act")
                def _(a, s_=s_, m_stt=m_stt, w_slotfree=w_slotfree):
                    a.wait_ge(sdve.h, m_stt)
                    if w_slotfree:
                        a.wait_ge(spe.h, w_slotfree)
                    a.activation(ptst[:, s_, :], stmp[s_], AF.Exp)\
                        .then_inc(sact.h, 1)
                m_e = sact.inc()
                slot_exp[slot_ctr] = m_e

                @op("pe")
                def _(t, h=h, qc=qc, kts=kts, s_=s_, even=even, m_e=m_e,
                      first_av=first_av, nkt=nkt, w_pyfree=prev_tt):
                    t.wait_ge(sact.h, m_e)
                    if first_av and w_pyfree:
                        t.wait_ge(sdve.h, w_pyfree)
                    for kt in kts:
                        c0 = 128 * (kt - 4 * qc) if kt >= nkt - 4 else 0
                        rhs = ptst[:, s_, (kt % 2) * 512 + c0:(kt % 2) * 512 + 512]
                        st_, sp_ = (kt == 0), (kt == nkt - 1)
                        if even:
                            ii = t.matmul(pY[0:65, c0:512], vaug[:, kt, h, 0:65], rhs,
                                          start=st_, stop=sp_)
                        else:
                            t.matmul(pT[64:128, 512 + c0:1024], vaug[:, kt, h, 0:64],
                                     rhs, start=st_, stop=sp_)
                            ii = t.matmul(pT[64:65, c0:512], vaug[:, kt, h, 64:65],
                                          rhs, start=st_, stop=sp_)
                    ii.then_inc(spe.h, 1)
                m_av = spe.inc()
                slot_av[slot_ctr] = m_av
                slot_ctr += 1
                first_av = False

            @op("dve")
            def _(v, even=even, m_av=slot_av[slot_ctr - 1]):
                v.wait_ge(spe.h, m_av)
                src = pY[64:65, 0:512] if even else pT[64:65, 0:512]
                v.reciprocal(rowall[64:65, 0:512], src).then_inc(sdve.h, 1)
            m_rc = sdve.inc()

            @op("pe")
            def _(t, even=even, m_rc=m_rc, w_pz=prev_bcc):
                t.wait_ge(sdve.h, m_rc)
                if w_pz:
                    t.wait_ge(sdve.h, w_pz)
                dst = pZ[0:64, 0:512] if even else pZ[64:128, 0:512]
                t.matmul(dst, ones_f[64:65, 0:64], rowall[64:65, 0:512],
                         start=True, stop=True).then_inc(spe.h, 1)
            m_bc = spe.inc()

            chunk = (qc // 4) * 4 + h // 2
            colo = (qc % 4) * 512

            @op("dve")
            def _(v, even=even, chunk=chunk, colo=colo, m_bc=m_bc):
                v.wait_ge(spe.h, m_bc)
                rows = slice(0, 64) if even else slice(64, 128)
                v.tensor_copy(bcsb[rows, 0:512], pZ[rows, 0:512]).then_inc(sdve.h, 1)
                src = pY[0:64, 0:512] if even else pT[64:128, 512:1024]
                v.tensor_tensor(yy[rows, chunk, colo:colo + 512], src,
                                bcsb[rows, 0:512], ALU.mult).then_inc(sdve.h, 1)
            prev_bcc = sdve.inc()
            prev_tt = sdve.inc()
    m_attn = prev_tt

    @op("pool")
    def _(g, w=m_attn):
        g.wait_ge(sdve.h, w)
        g.dma_start(out=pidx, in_=pidxt.ap()).then_inc(dgp.h, 16)
        for j in range(8):
            g.dma_start(out=yx_in.ap()[j * 128:(j + 1) * 128, :],
                        in_=yy[:, j, :]).then_inc(dgp.h, 16)
    m_ysend = dgp.inc(144)

    @op("dve")
    def _(v, w=m_attn):
        for fl in range(4):
            src0 = yy[:, fl, :]
            v.reg_add(p4regs["ro"], p4regs["rp"], src0.offset)
            sap = bass.AP(src0.tensor, p4regs["ro"], [list(p) for p in src0.ap])
            ii = v.tensor_copy(yproj[:, fl, :], sap)
        ii.then_inc(sdve.h, 1)
    m_yproj = sdve.inc()

    @op("pool")
    def _(g):
        g.wait_ge(dgp.h, m_ysend)
        g.collective_compute(
            "AllGather", ALU.bypass, replica_groups=RG,
            ins=[yx_in.ap().opt()], outs=[yx_out.ap().opt()],
        ).then_inc(scy.h, 1)
    scy.inc()

    @op("pool")
    def _(g):
        g.wait_ge(scy.h, 1)
        for fl in range(4):
            g.indirect_dma_start(
                out=yrecv[:, fl, :], out_offset=None,
                in_=yx_out.ap(),
                in_offset=bass.IndirectOffsetOnAxis(ap=pidx[:, fl:fl + 1], axis=0),
            ).then_inc(dgp.h, 16)
    m_yrecv = dgp.inc(64)

    if stage < 5:
        return _emit(nc, es, PL, dbg, dict(locals())), dbg

    p5_mm, p5_ev, p5_st = [0] * NTO, [0] * NTO, [0] * NTO
    for tt in range(NTO):
        @op("sync")
        def _(s, tt=tt, wv=(p5_ev[tt - 2] if tt >= 2 else 0),
              wd=(p5_st[tt - 2] if tt >= 2 else 0)):
            if tt < 2:
                s.wait_ge(sdve.h, m_attn)
            if wv:
                s.wait_ge(sdve.h, wv)
            if wd:
                s.wait_ge(dgp.h, wd)
            s.dma_start(out=xt7[tt % 2], in_=xot.ap()[tt * 128:(tt + 1) * 128, :])\
                .then_inc(dsy.h, 16)
        m_x = dsy.inc(16)

        @op("pe")
        def _(t, tt=tt, w=(p5_ev[tt - 2] if tt >= 2 else 0)):
            if tt == 0:
                t.wait_ge(dgp.h, m_yrecv)
                t.wait_ge(sdve.h, m_yproj)
            if w:
                t.wait_ge(sdve.h, w)
            bb = (tt % 2) * 1024
            for fc in range(8):
                ysrc = yproj if fc < 4 else yrecv
                for ns in range(2):
                    ii = t.matmul(pS[:, bb + ns * 512:bb + ns * 512 + 512],
                                  ysrc[:, fc % 4, tt * 128:tt * 128 + 128],
                                  wpT[:, fc, ns * 512:ns * 512 + 512],
                                  start=(fc == 0), stop=(fc == 7))
            ii.then_inc(spe.h, 1)
        p5_mm[tt] = spe.inc()

        @op("dve")
        def _(v, tt=tt, m_mm=p5_mm[tt], m_x=m_x):
            v.wait_ge(spe.h, m_mm)
            v.wait_ge(dsy.h, m_x)
            bb = (tt % 2) * 1024
            v.scalar_tensor_tensor(xt7[tt % 2], pS[:, bb:bb + 1024], 1.0,
                                   xt7[tt % 2], ALU.mult, ALU.add)\
                .then_inc(sdve.h, 1)
        p5_ev[tt] = sdve.inc()

        @op("pool")
        def _(g, tt=tt, m_ev=p5_ev[tt]):
            g.wait_ge(sdve.h, m_ev)
            g.dma_start(out=outt.ap()[tt * 128:(tt + 1) * 128, :], in_=xt7[tt % 2])\
                .then_inc(dgp.h, 16)
        p5_st[tt] = dgp.inc(16)
    m_p5 = p5_st[-1]

    if stage < 7:
        return _emit(nc, es, PL, dbg, dict(locals())), dbg

    def dequant(slab, dq, m_slab, w_dqfree, extra_dve_waits=()):
        @op("dve")
        def _(v, slab=slab, dq=dq, m=m_slab, wf=w_dqfree, ew=tuple(extra_dve_waits),
              base=sdve.n):
            v.wait_ge(dsy.h, m)
            if wf:
                v.wait_ge(spe.h, wf)
            for sem_, val_ in ew:
                v.wait_ge(sem_.h, val_)
            v.tensor_reduce(dqs[:, 0:32, None],
                            slab.rearrange("p (b e) -> p b e", e=32),
                            AX.X, ALU.max, apply_absolute_value=True)\
                .then_inc(sdve.h, 1)
            v.wait_ge(sdve.h, base + 1)
            v.tensor_scalar(dqs[:, 0:32], dqs[:, 0:32], 1.0 / QW, 1e-12,
                            ALU.mult, ALU.max).then_inc(sdve.h, 1)
            v.wait_ge(sdve.h, base + 2)
            v.reciprocal(dqi[:, 0:32], dqs[:, 0:32]).then_inc(sdve.h, 1)
            v.wait_ge(sdve.h, base + 3)
            slab3 = slab.rearrange("p (b e) -> p b e", e=32)
            dq3 = dq.rearrange("p (b e) -> p b e", e=32)
            v.tensor_tensor(slab3, slab3,
                            dqi[:, 0:32, None].broadcast_to((128, 32, 32)), ALU.mult)
            v.tensor_scalar(slab, slab, RNE_C, RNE_C, ALU.add, ALU.subtract)
            v.tensor_tensor(dq3, slab3,
                            dqs[:, 0:32, None].broadcast_to((128, 32, 32)), ALU.mult)\
                .then_inc(sdve.h, 1)
        return sdve.inc(4)

    ev_done = {}
    dq_free = {}
    wtile_free = {}
    mx_done = {}
    spill_m = {}
    slab_consumed = [0] * (2 * NI)

    for wi, wsrc in enumerate([wupt, wgatet]):
        for ci in range(NI):
            k = wi * NI + ci

            @op("sync")
            def _(s, wsrc=wsrc, ci=ci, k=k,
                  w=(slab_consumed[k - 2] if k >= 2 else 0),
                  wser=((mx_done[("mx", k - 1)], mx_done[("ev", k - 1, 1)],
                         mm_m if False else 0) if (P7SER and k >= 1) else 0)):
                if k == 0:
                    s.wait_ge(dgp.h, m_p5)
                if wser:
                    s.wait_ge(sdve.h, wser[0])
                    s.wait_ge(sact.h, wser[1])
                if w:
                    s.wait_ge(sdve.h, w)
                s.dma_start(out=wslab[k % 2],
                            in_=wsrc.ap()[ci * 128:(ci + 1) * 128, :])\
                    .then_inc(dsy.h, 16)
            m_slab = dsy.inc(16)

            extra = []
            if wi == 0 and ci >= 2:
                extra.append((dgp, spill_m[ci - 2]))
            m_dq = dequant(wslab[k % 2], dqtile[k % 2], m_slab,
                           dq_free.get(k - 2, 0), extra)
            slab_consumed[k] = m_dq

            m_ev_prev = ev_done.get(k - 1, 0)

            @op("pe")
            def _(t, k=k, m_dq=m_dq, wev=m_ev_prev):
                t.wait_ge(sdve.h, m_dq)
                if wev:
                    t.wait_ge(sdve.h, wev)
                for cc in range(4):
                    ii = t.transpose(BANK4[cc],
                                     dqtile[k % 2][:, cc * 128:cc * 128 + 128], ident)
                ii.then_inc(spe.h, 1)
            m_tr0 = spe.inc()

            @op("dve")
            def _(v, k=k, m_tr0=m_tr0):
                v.wait_ge(spe.h, m_tr0)
                v.tensor_copy(wtile[k % 2][:, 0:128], pYb[:, 0:128])
                v.tensor_copy(wtile[k % 2][:, 128:256], pZb[:, 0:128])
                v.tensor_copy(wtile[k % 2][:, 256:384], pTb[:, 0:128])
                v.tensor_copy(wtile[k % 2][:, 384:512], pTb[:, 1024:1152])\
                    .then_inc(sdve.h, 1)
            m_ev0 = sdve.inc()

            @op("pe")
            def _(t, k=k, m_ev0=m_ev0):
                t.wait_ge(sdve.h, m_ev0)
                for cc in range(4):
                    ii = t.transpose(
                        BANK4[cc],
                        dqtile[k % 2][:, 512 + cc * 128:512 + cc * 128 + 128], ident)
                ii.then_inc(spe.h, 1)
            m_tr = spe.inc()
            dq_free[k] = m_tr

            @op("dve")
            def _(v, k=k, m_tr=m_tr):
                v.wait_ge(spe.h, m_tr)
                v.tensor_copy(wtile[k % 2][:, 512:640], pYb[:, 0:128])
                v.tensor_copy(wtile[k % 2][:, 640:768], pZb[:, 0:128])
                v.tensor_copy(wtile[k % 2][:, 768:896], pTb[:, 0:128])
                v.tensor_copy(wtile[k % 2][:, 896:1024], pTb[:, 1024:1152])\
                    .then_inc(sdve.h, 1)
            m_wt = sdve.inc()
            ev_done[k] = m_wt

            mm_m = {}
            for hf in range(2):
                @op("pe")
                def _(t, k=k, hf=hf, m_wt=m_wt,
                      w_ev=(mx_done.get(("ev", k - 1, hf), 0))):
                    t.wait_ge(sdve.h, m_wt)
                    if k == 0 and hf == 0:
                        t.wait_ge(sdve.h, m_xq)
                    if w_ev:
                        t.wait_ge(sact.h, w_ev)
                    for cc in range(8):
                        for ns in range(2):
                            ii = t.matmul(
                                pS[:, hf * 1024 + ns * 512:hf * 1024 + ns * 512 + 512],
                                wtile[k % 2][:, cc * 128:cc * 128 + 128],
                                xqT[:, cc, hf * 1024 + ns * 512:hf * 1024 + ns * 512 + 512],
                                start=(cc == 0), stop=(cc == 7))
                    ii.then_inc(spe.h, 1)
                mm_m[hf] = spe.inc()

                dst = upevA[ci % 2] if wi == 0 else graw(ci)

                @op("act")
                def _(a, dst=dst, hf=hf, m_mm=mm_m[hf]):
                    a.wait_ge(spe.h, m_mm)
                    a.copy(dst[:, hf * 1024:hf * 1024 + 1024],
                           pS[:, hf * 1024:hf * 1024 + 1024]).then_inc(sact.h, 1)
                mx_done[("ev", k, hf)] = sact.inc()


            dst = upevA[ci % 2] if wi == 0 else graw(ci)
            col = wi

            @op("dve")
            def _(v, dst=dst, col=col, e1=mx_done[("ev", k, 1)], base=sdve.n):
                v.wait_ge(sact.h, e1)
                v.tensor_reduce(lnm[:, 0:1], dst, AX.X, ALU.max,
                                apply_absolute_value=True).then_inc(sdve.h, 1)
                v.wait_ge(sdve.h, base + 1)
                v.tensor_tensor(runmax[:, col:col + 1], runmax[:, col:col + 1],
                                lnm[:, 0:1], ALU.max).then_inc(sdve.h, 1)
            mx_done[("mx", k)] = sdve.inc(2)

            if wi == 0:
                @op("pool")
                def _(g, ci=ci, e1=mx_done[("ev", k, 1)]):
                    g.wait_ge(sact.h, e1)
                    g.dma_start(out=upspill.ap()[ci * 128:(ci + 1) * 128, :],
                                in_=upevA[ci % 2]).then_inc(dgp.h, 16)
                spill_m[ci] = dgp.inc(16)


    m_gate_done = mx_done[("mx", 2 * NI - 1)]
    m_up_done = mx_done[("mx", NI - 1)]
    m_p7act = mx_done[("ev", 2 * NI - 1, 1)]

    def runmax_collect(idx, cols, w_val, sc_dst):
        ncols = len(cols)

        @op("dve")
        def _(v, w=w_val, cols=tuple(cols)):
            v.wait_ge(sdve.h, w)
            v.tensor_copy(mxb[:, 0:ncols], runmax[:, cols[0]:cols[0] + ncols])\
                .then_inc(sdve.h, 1)
        m0_ = sdve.inc()

        @op("pe")
        def _(t, m0_=m0_):
            t.wait_ge(sdve.h, m0_)
            t.transpose(pTb[0:ncols, 0:128], mxb[:, 0:ncols], ident)\
                .then_inc(spe.h, 1)
        m1 = spe.inc()

        @op("dve")
        def _(v, m1=m1, base=sdve.n):
            v.wait_ge(spe.h, m1)
            v.tensor_reduce(norm8[0:ncols, 0:1], pTb[0:ncols, 0:128], AX.X, ALU.max)\
                .then_inc(sdve.h, 1)
            v.wait_ge(sdve.h, base + 1)
            v.tensor_copy(mxb[0:ncols, 1:2], norm8[0:ncols, 0:1]).then_inc(sdve.h, 1)
        m2 = sdve.inc(2)

        @op("pe")
        def _(t, m2=m2):
            t.wait_ge(sdve.h, m2)
            t.transpose(pTb[0:1, 256:256 + ncols], mxb[0:ncols, 1:2],
                        ident[0:ncols, 0:ncols]).then_inc(spe.h, 1)
        m3 = spe.inc()

        @op("dve")
        def _(v, m3=m3):
            v.wait_ge(spe.h, m3)
            v.tensor_copy(scalars[:, sc_dst:sc_dst + ncols], pTb[0:1, 256:256 + ncols])\
                .then_inc(sdve.h, 1)
        return sdve.inc()

    m_rm2 = runmax_collect(1, [0, 1], m_gate_done, SC_T0)
    m_c2 = collective(1, scalars[:, SC_T0:SC_T0 + 2], 2, sdve, m_rm2)

    @op("dve")
    def _(v, w=m_c2, base=sdve.n):
        v.wait_ge(dgp.h, w)
        g8 = scalars[:, SC_G:SC_G + 16].rearrange("o (r c) -> o c r", r=8)
        v.tensor_reduce(scalars[:, SC_MUP:SC_MUP + 1], g8[0:1, 0:1, :], AX.X, ALU.max)
        v.tensor_reduce(scalars[:, SC_MGATE:SC_MGATE + 1], g8[0:1, 1:2, :],
                        AX.X, ALU.max).then_inc(sdve.h, 1)
        v.wait_ge(sdve.h, base + 1)
        v.tensor_scalar(scalars[:, SC_SC + 1:SC_SC + 2], scalars[:, SC_MUP:SC_MUP + 1],
                        1.01 / QA, 1e-12, ALU.mult, ALU.max)
        v.tensor_scalar(scalars[:, SC_SC + 3:SC_SC + 4],
                        scalars[:, SC_MGATE:SC_MGATE + 1],
                        1.01 / QA, 1e-12, ALU.mult, ALU.max).then_inc(sdve.h, 1)
        v.wait_ge(sdve.h, base + 2)
        v.reciprocal(scalars[:, SC_SC:SC_SC + 1], scalars[:, SC_SC + 1:SC_SC + 2])
        v.reciprocal(scalars[:, SC_SC + 2:SC_SC + 3], scalars[:, SC_SC + 3:SC_SC + 4])\
            .then_inc(sdve.h, 1)
    m_sc2 = sdve.inc(3)

    @op("pe")
    def _(t, w=m_sc2):
        t.wait_ge(sdve.h, w)
        t.matmul(pZ[:, 0:4], ones_f[0:1, 0:128], scalars[:, SC_SC:SC_SC + 4],
                 start=True, stop=True).then_inc(spe.h, 1)
    m_bc2 = spe.inc()

    @op("dve")
    def _(v, w=m_bc2):
        v.wait_ge(spe.h, w)
        v.tensor_copy(bcast[:, BC_INVU:BC_INVU + 4], pZ[:, 0:4]).then_inc(sdve.h, 1)
    m_bc2c = sdve.inc()

    if stage < 9:
        return _emit(nc, es, PL, dbg, dict(locals())), dbg

    wd_list = [(ip, rc) for ip in range(3) for rc in range(8)]
    chain_m = {}
    wd_m = {}
    wq = 0

    prev_u2 = 0
    wd_slab_consumed = {}
    wd_dq_free = {}
    wd_tr = {}
    for ci in range(NI):
        @op("dve")
        def _(v, ci=ci, w=m_bc2c):
            if ci == 0:
                v.wait_ge(sdve.h, w)
                v.wait_ge(dgp.h, m_yrecv)
            v.tensor_scalar(scr[0], graw(ci), bcast[:, BC_INVG:BC_INVG + 1], RNE_C,
                            ALU.mult, ALU.add)
            v.tensor_scalar(scr[1], scr[0], RNE_C, bcast[:, BC_SG:BC_SG + 1],
                            ALU.subtract, ALU.mult).then_inc(sdve.h, 1)
        m_gq = sdve.inc()

        @op("act")
        def _(a, m_gq=m_gq):
            a.wait_ge(sdve.h, m_gq)
            a.activation(scr[2], scr[1], AF.Sigmoid)
            a.activation(scr[0], scr[2], AF.Identity, scale=65536.0,
                         bias=cbias[:, 1:2])
            a.activation(scr[2], scr[0], AF.Identity, scale=1.0 / 65536.0,
                         bias=cbias[:, 2:3]).then_inc(sact.h, 1)
        m_u2 = sact.inc()

        @op("dve")
        def _(v, ci=ci, m_u2=m_u2):
            v.wait_ge(sact.h, m_u2)
            v.scalar_tensor_tensor(graw(ci, F16), scr[2], U16_CLIP, scr[1],
                                   ALU.min, ALU.mult).then_inc(sdve.h, 1)
        m_g2 = sdve.inc()

        @op("dve")
        def _(v, ci=ci, base=sdve.n):
            v.tensor_reduce(lnm[:, 1:2], graw(ci, F16), AX.X, ALU.max,
                            apply_absolute_value=True).then_inc(sdve.h, 1)
            v.wait_ge(sdve.h, base + 1)
            v.tensor_tensor(runmax[:, 2:3], runmax[:, 2:3], lnm[:, 1:2], ALU.max)\
                .then_inc(sdve.h, 1)
        chain_m[ci] = sdve.inc(2)

        if wq < len(wd_list):
            ip, rc = wd_list[wq]

            @op("sync")
            def _(s, ip=ip, rc=rc, wq=wq, w=(wd_slab_consumed.get(wq - 2, 0)),
                  w0=(slab_consumed[2 * NI - 1] if wq < 2 else 0)):
                if w0:
                    s.wait_ge(sdve.h, w0)
                if w:
                    s.wait_ge(sdve.h, w)
                s.dma_start(out=wslab[wq % 2],
                            in_=wdownt.ap()[rc * 128:(rc + 1) * 128,
                                            ip * 1024:(ip + 1) * 1024])\
                    .then_inc(dsy.h, 16)
            m_slab = dsy.inc(16)

            m_dq = dequant(wslab[wq % 2], wd_dqt[wq % 2], m_slab,
                           wd_dq_free.get(wq - 2, 0))
            wd_slab_consumed[wq] = m_dq

            @op("pe")
            def _(t, wq=wq, m_dq=m_dq, wev=(wd_tr.get(wq - 1, 0))):
                t.wait_ge(sdve.h, m_dq)
                if wq == 0:
                    t.wait_ge(sact.h, m_p7act)
                if wev:
                    t.wait_ge(sdve.h, wev)
                for j in range(8):
                    ii = t.transpose(BANK8[j],
                                     wd_dqt[wq % 2][:, j * 128:j * 128 + 128], ident)
                ii.then_inc(spe.h, 1)
            m_tr = spe.inc()
            wd_dq_free[wq] = m_tr

            @op("dve")
            def _(v, ip=ip, rc=rc, wq=wq, m_tr=m_tr,
                  wsp=(spill_m[NI - 1] if ip == 2 and rc == 0 else 0)):
                v.wait_ge(spe.h, m_tr)
                if wsp:
                    v.wait_ge(dgp.h, wsp)
                rng = slice(rc * 128, rc * 128 + 128)
                dstg = wdT_a[:, ip * 8:, :] if ip < 2 else wdT_b
                for j in range(8):
                    ii = v.tensor_copy(dstg[:, j, rng], BANK8[j])
                ii.then_inc(sdve.h, 1)
            wd_tr[wq] = sdve.inc()
            wq += 1
    m_wd_done = wd_tr[wq - 1]
    m_chain = chain_m[NI - 1]

    m_rm3 = runmax_collect(2, [2], m_chain, SC_T0)
    m_c3 = collective(2, scalars[:, SC_T0:SC_T0 + 2], 2, sdve, m_rm3)

    @op("dve")
    def _(v, w=m_c3, base=sdve.n):
        v.wait_ge(dgp.h, w)
        v.tensor_reduce(scalars[:, SC_MG2:SC_MG2 + 1],
                        scalars[:, SC_G:SC_G + 16]
                            .rearrange("o (r c) -> o c r", r=8)[:, 0:1, :],
                        AX.X, ALU.max).then_inc(sdve.h, 1)
        v.wait_ge(sdve.h, base + 1)
        v.tensor_scalar(scalars[:, SC_SC + 5:SC_SC + 6], scalars[:, SC_MG2:SC_MG2 + 1],
                        1.01 / QA, 1e-12, ALU.mult, ALU.max).then_inc(sdve.h, 1)
        v.wait_ge(sdve.h, base + 2)
        v.reciprocal(scalars[:, SC_SC + 4:SC_SC + 5], scalars[:, SC_SC + 5:SC_SC + 6])\
            .then_inc(sdve.h, 1)
    m_sc3 = sdve.inc(3)

    @op("pe")
    def _(t, w=m_sc3):
        t.wait_ge(sdve.h, w)
        t.matmul(pZ[:, 0:2], ones_f[0:1, 0:128], scalars[:, SC_SC + 4:SC_SC + 6],
                 start=True, stop=True).then_inc(spe.h, 1)
    m_bc3 = spe.inc()

    @op("dve")
    def _(v, w=m_bc3):
        v.wait_ge(spe.h, w)
        v.tensor_copy(bcast[:, BC_INV3:BC_INV3 + 2], pZ[:, 0:2]).then_inc(sdve.h, 1)
    m_bc3c = sdve.inc()

    if stage < 10:
        return _emit(nc, es, PL, dbg, dict(locals())), dbg

    oraw_m = {}
    up_rb = {}
    for ci in range(NI):
        @op("sync")
        def _(s, ci=ci, w=(oraw_m.get(ci - 2, 0)),
              w0=(spill_m[NI - 1] if ci == 0 else 0)):
            if w0:
                s.wait_ge(dgp.h, w0)
            if w:
                s.wait_ge(sdve.h, w)
            s.dma_start(out=upevB[ci % 2],
                        in_=upspill.ap()[ci * 128:(ci + 1) * 128, :])\
                .then_inc(dsy.h, 16)
        up_rb[ci] = dsy.inc(16)

        @op("act")
        def _(a, ci=ci, m=up_rb[ci], w=m_bc3c,
              wprev=(oraw_m.get(ci - 1, 0))):
            a.wait_ge(dsy.h, m)
            if ci == 0:
                a.wait_ge(sdve.h, w)
            if wprev:
                a.wait_ge(sdve.h, wprev)
            a.activation(scr[0], upevB[ci % 2], AF.Identity,
                         scale=bcast[:, BC_INVU:BC_INVU + 1], bias=cbias[:, 1:2])
            a.activation(scr[2], graw(ci, F16), AF.Identity,
                         scale=bcast[:, BC_INV3:BC_INV3 + 1], bias=cbias[:, 1:2])\
                .then_inc(sact.h, 1)
        m_tu = sact.inc()

        @op("dve")
        def _(v, ci=ci, m_tu=m_tu, w=m_bc3c):
            v.wait_ge(sact.h, m_tu)
            if ci == 0:
                v.wait_ge(sdve.h, w)
            v.tensor_scalar(scr[0], scr[0], RNE_C, bcast[:, BC_SU:BC_SU + 1],
                            ALU.subtract, ALU.mult)
            v.tensor_scalar(scr[2], scr[2], RNE_C, bcast[:, BC_S3:BC_S3 + 1],
                            ALU.subtract, ALU.mult)
            v.tensor_tensor(graw(ci, F16), scr[0], scr[2], ALU.mult)\
                .then_inc(sdve.h, 1)
        oraw_m[ci] = sdve.inc()

        @op("dve")
        def _(v, ci=ci, base=sdve.n):
            v.tensor_reduce(lnm[:, 2:3], graw(ci, F16), AX.X, ALU.max,
                            apply_absolute_value=True).then_inc(sdve.h, 1)
            v.wait_ge(sdve.h, base + 1)
            v.tensor_tensor(runmax[:, 3:4], runmax[:, 3:4], lnm[:, 2:3], ALU.max)\
                .then_inc(sdve.h, 1)
        sdve.inc(2)
    m_ow = sdve.n

    m_rm4 = runmax_collect(3, [3], m_ow, SC_T0)
    m_c4 = collective(3, scalars[:, SC_T0:SC_T0 + 2], 2, sdve, m_rm4)

    @op("dve")
    def _(v, w=m_c4, base=sdve.n):
        v.wait_ge(dgp.h, w)
        v.tensor_reduce(scalars[:, SC_MO:SC_MO + 1],
                        scalars[:, SC_G:SC_G + 16]
                            .rearrange("o (r c) -> o c r", r=8)[:, 0:1, :],
                        AX.X, ALU.max).then_inc(sdve.h, 1)
        v.wait_ge(sdve.h, base + 1)
        v.tensor_scalar(scalars[:, SC_SC + 7:SC_SC + 8], scalars[:, SC_MO:SC_MO + 1],
                        1.01 / QA, 1e-12, ALU.mult, ALU.max).then_inc(sdve.h, 1)
        v.wait_ge(sdve.h, base + 2)
        v.reciprocal(scalars[:, SC_SC + 6:SC_SC + 7], scalars[:, SC_SC + 7:SC_SC + 8])\
            .then_inc(sdve.h, 1)
    m_sc4 = sdve.inc(3)

    @op("pe")
    def _(t, w=m_sc4):
        t.wait_ge(sdve.h, w)
        t.matmul(pZ[:, 0:2], ones_f[0:1, 0:128], scalars[:, SC_SC + 6:SC_SC + 8],
                 start=True, stop=True).then_inc(spe.h, 1)
    m_bc4 = spe.inc()

    @op("dve")
    def _(v, w=m_bc4):
        v.wait_ge(spe.h, w)
        v.tensor_copy(bcast[:, BC_INVO:BC_INVO + 2], pZ[:, 0:2]).then_inc(sdve.h, 1)
    m_bc4c = sdve.inc()

    oq_m = {}
    for ci in range(NI):
        @op("act")
        def _(a, ci=ci, w=m_bc4c, wprev=(oq_m.get(ci - 1, 0))):
            if ci == 0:
                a.wait_ge(sdve.h, w)
            if wprev:
                a.wait_ge(sdve.h, wprev)
            a.activation(scr[0], graw(ci, F16), AF.Identity,
                         scale=bcast[:, BC_INVO:BC_INVO + 1], bias=cbias[:, 1:2])\
                .then_inc(sact.h, 1)
        m_t = sact.inc()

        @op("dve")
        def _(v, ci=ci, m_t=m_t):
            v.wait_ge(sact.h, m_t)
            v.tensor_scalar(graw(ci), scr[0], RNE_C, bcast[:, BC_SO:BC_SO + 1],
                            ALU.subtract, ALU.mult).then_inc(sdve.h, 1)
        oq_m[ci] = sdve.inc()
    m_oq = oq_m[NI - 1]

    d_mm, d_ev, d_st = [0] * NTO, [0] * NTO, [0] * NTO
    for tt in range(NTO):
        @op("sync")
        def _(s, tt=tt, w=(d_ev[tt - 2] if tt >= 2 else 0)):
            if tt < 2:
                s.wait_ge(spe.h, wd_dq_free[len(wd_list) - 1])
            if w:
                s.wait_ge(sdve.h, w)
            s.dma_start(out=xt10[tt % 2], in_=outt.ap()[tt * 128:(tt + 1) * 128, :])\
                .then_inc(dsy.h, 16)
        m_x = dsy.inc(16)

        @op("pe")
        def _(t, tt=tt, w=(d_ev[tt - 2] if tt >= 2 else 0)):
            if tt == 0:
                t.wait_ge(sdve.h, m_oq)
                t.wait_ge(sdve.h, m_wd_done)
            if w:
                t.wait_ge(sdve.h, w)
            bb = (tt % 2) * 1024
            for ci in range(NI):
                for ns in range(2):
                    ii = t.matmul(pS[:, bb + ns * 512:bb + ns * 512 + 512],
                                  graw(ci)[:, tt * 128:tt * 128 + 128],
                                  wdT(ci)[:, ns * 512:ns * 512 + 512],
                                  start=(ci == 0), stop=(ci == NI - 1))
            ii.then_inc(spe.h, 1)
        d_mm[tt] = spe.inc()

        @op("dve")
        def _(v, tt=tt, m_mm=d_mm[tt], m_x=m_x, w=(d_st[tt - 2] if tt >= 2 else 0)):
            v.wait_ge(spe.h, m_mm)
            v.wait_ge(dsy.h, m_x)
            if w:
                v.wait_ge(dgp.h, w)
            bb = (tt % 2) * 1024
            v.scalar_tensor_tensor(ot10[tt % 2], pS[:, bb:bb + 1024], 1.0,
                                   xt10[tt % 2], ALU.mult, ALU.add)\
                .then_inc(sdve.h, 1)
        d_ev[tt] = sdve.inc()

        @op("pool")
        def _(g, tt=tt, m_ev=d_ev[tt]):
            g.wait_ge(sdve.h, m_ev)
            g.dma_start(out=outt.ap()[tt * 128:(tt + 1) * 128, :], in_=ot10[tt % 2])\
                .then_inc(dgp.h, 16)
        d_st[tt] = dgp.inc(16)

    return _emit(nc, es, PL, dbg, dict(locals())), dbg


def _emit(nc, es, PL, dbg, env):
    dgp, dsy = env["dgp"], env["dsy"]
    sdve, sact, spool = env["sdve"], env["sact"], env["spool"]

    def op(engine):
        def deco(fn):
            PL[engine].append(fn)
        return deco

    marks = dict(sdve=sdve.n, sact=sact.n, spe=env["spe"].n)
    if dbg:
        pairs = []
        if "xnT" in dbg:
            pairs.append((dbg["xnT"].ap()[:, 0:4 * TOK],
                          env["xnT_lo"].rearrange("p c t -> p (c t)")))
            pairs.append((dbg["xnT"].ap()[:, 4 * TOK:8 * TOK],
                          env["xnT_hi"].rearrange("p c t -> p (c t)")))
        if "qT" in dbg:
            pairs.append((dbg["qT"].ap(), env["qT"].rearrange("p c t -> p (c t)")))
        if "kT" in dbg:
            pairs.append((dbg["kT"].ap(), env["kT"].rearrange("p c t -> p (c t)")))
        if "v" in dbg:
            pairs.append((dbg["v"].ap(),
                          env["vaug"].rearrange("p a b c -> p (a b c)")))
        if "xq" in dbg:
            pairs.append((dbg["xq"].ap(), env["xqT"].rearrange("p c t -> p (c t)")))
        if "yy" in dbg:
            pairs.append((dbg["yy"].ap(), env["yy"].rearrange("p c t -> p (c t)")))
        if "gr" in dbg:
            for ci in range(NI):
                pairs.append((dbg["gr"].ap()[:, ci * TOWN:(ci + 1) * TOWN],
                              env["graw"](ci)))
        if "oraw" in dbg:
            for ci in range(NI):
                pairs.append((dbg["oraw"].ap()[:, ci * TOWN:(ci + 1) * TOWN],
                              env["graw"](ci, F16)))
        if "wdt" in dbg:
            for ci in range(NI):
                pairs.append((dbg["wdt"].ap()[:, ci * CD:(ci + 1) * CD],
                              env["wdT"](ci)))
        if "dqsd" in dbg:
            pairs.append((dbg["dqsd"].ap()[:, 0:32], env["dqs"]))
            pairs.append((dbg["dqsd"].ap()[:, 32:64], env["dqi"]))
        if "wsl" in dbg:
            pairs.append((dbg["wsl"].ap()[:, 0:1024], env["wslab"][0]))
            pairs.append((dbg["wsl"].ap()[:, 1024:2048], env["wslab"][1]))
        if "wtl" in dbg:
            pairs.append((dbg["wtl"].ap()[:, 0:1024], env["wtile"][0]))
            pairs.append((dbg["wtl"].ap()[:, 1024:2048], env["wtile"][1]))
            pairs.append((dbg["wtl"].ap()[:, 2048:3072], env["dqtile"][0]))
            pairs.append((dbg["wtl"].ap()[:, 3072:4096], env["dqtile"][1]))
        if "wlt" in dbg:
            pairs.append((dbg["wlt"].ap(), env["wlT"].rearrange("p a b -> p (a b)")))
        if "scal" in dbg:
            pairs.append((dbg["scal"].ap(), env["scalars"]))
        if "rmx" in dbg:
            pairs.append((dbg["rmx"].ap()[:, 0:8], env["runmax"]))
            pairs.append((dbg["rmx"].ap()[:, 8:24], env["bcast"]))
        if "lnm" in dbg:
            pairs.append((dbg["lnm"].ap()[:, 0:32], env["negm_t"]))
            pairs.append((dbg["lnm"].ap()[:, 32:64], env["rstd_t"]))
            pairs.append((dbg["lnm"].ap()[:, 64:96], env["amax_t"]))

        @op("pool")
        def _(g, pairs=pairs):
            g.wait_ge(sdve.h, marks["sdve"])
            g.wait_ge(sact.h, marks["sact"])
            g.wait_ge(env["spe"].h, marks["spe"])
            for d, s in pairs:
                g.dma_start(out=d, in_=s).then_inc(dgp.h, 16)
        dgp.inc(16 * len(pairs))

    tot_gp, tot_sy = dgp.n, dsy.n

    @op("pool")
    def _(g):
        g.wait_ge(dgp.h, tot_gp)
        g.wait_ge(dsy.h, tot_sy)

    with nc.Block() as block:
        block.sync(lambda e: [f(e) for f in PL["sync"]])
        block.tensor(lambda e: [f(e) for f in PL["pe"]])
        block.scalar(lambda e: [f(e) for f in PL["act"]])
        block.vector(lambda e: [f(e) for f in PL["dve"]])
        block.gpsimd(lambda e: [f(e) for f in PL["pool"]])
    es.close()
    return nc


def _prep(inputs):
    for dead in ("L0_attn_pos_emb", "L0_attn_W_fc", "L0_attn_W_cproj",
                 "L0_attn_ln_w"):
        inputs.get(dead)
    x = np.ascontiguousarray(np.asarray(inputs["L0_attn_x"], dtype=np.float32))
    Wa = np.ascontiguousarray(np.asarray(inputs["L0_attn_W_attn"], dtype=np.float32))
    Wp = np.ascontiguousarray(np.asarray(inputs["L0_attn_W_proj"], dtype=np.float32))
    wg = np.ascontiguousarray(np.asarray(inputs["L0_ffn_w_gate"], dtype=np.float32))
    wu = np.ascontiguousarray(np.asarray(inputs["L0_ffn_w_up"], dtype=np.float32))
    wd = np.ascontiguousarray(np.asarray(inputs["L0_ffn_w_down"], dtype=np.float32))
    maps = []
    for c in range(8):
        b, hg = c // 2, c % 2
        wqkv = np.concatenate([
            Wa[hg * 512:(hg + 1) * 512],
            Wa[1024 + hg * 512:1024 + (hg + 1) * 512],
            Wa[2048 + hg * 512:2048 + (hg + 1) * 512],
        ], axis=0)
        wproj_c = Wp if hg == 0 else np.ascontiguousarray(
            np.concatenate([Wp[:, 512:1024], Wp[:, 0:512]], axis=1))
        p_ = np.arange(128, dtype=np.int32)[:, None]
        fl_ = np.arange(4, dtype=np.int32)[None, :]
        pidx = ((c ^ 1) * 1024 + hg * 512 + fl_ * 128 + p_).astype(np.int32)
        maps.append(dict(
            xb=np.ascontiguousarray(x[b]),
            xo=np.ascontiguousarray(x[b, hg * TOWN:(hg + 1) * TOWN]),
            wqkv=np.ascontiguousarray(wqkv),
            wproj=wproj_c, wgate=wg, wup=wu, wdown=wd,
            pidx=pidx,
        ))
    return x, maps


def kernel(**inputs):
    x, maps = _prep(inputs)
    nc, _ = build(stage=10)
    res = run_bass_kernel_spmd(nc, maps, core_ids=list(range(8)))
    outx = np.empty((4, TOK, CD), dtype=np.float32)
    for c in range(8):
        b, hg = c // 2, c % 2
        outx[b, hg * TOWN:(hg + 1) * TOWN] = res.results[c]["out"]
    return outx
